# version 04
# baseline (speedup 1.0000x reference)
import numpy as np
import ml_dtypes
from contextlib import ExitStack, contextmanager
import concourse.bass as bass
import concourse.mybir as mybir
from concourse.bass_utils import run_bass_kernel_spmd

F32 = mybir.dt.float32
BF16 = mybir.dt.bfloat16
I32 = mybir.dt.int32
AF = mybir.ActivationFunctionType
ALU = mybir.AluOpType
AX = mybir.AxisListType

N_CORES = 8
L, D, H, DK = 4096, 1024, 4, 256
NT, NST, G, HID = 32, 8, 8, 3072

DEBUG = []

ENG2BLK = {"pe": "tensor", "dve": "vector", "act": "scalar",
           "pool": "gpsimd", "sync": "sync"}


class Res:
    __slots__ = ("name", "w_tok", "readers")

    def __init__(self, name):
        self.name = name
        self.w_tok = None
        self.readers = {}


class Prog:
    def __init__(self, nc):
        self.nc = nc
        self.steps = {e: [] for e in ENG2BLK}
        self.cnt = {e: 0 for e in ENG2BLK if e != "sync"}
        self.dma_cnt = 0
        self.hwm = {}
        self.in_stage = False
        self.touch = None
        self.emitted = None
        self.sems = {}
        self._res = {}
        self.dma_safe = {e: 0 for e in ("pe", "dve", "act", "pool")}

    def res(self, name):
        if name not in self._res:
            self._res[name] = Res(name)
        return self._res[name]

    def make_sems(self, es):
        for e in ("pe", "dve", "act", "pool", "dma", "dmadone"):
            self.sems[e] = es.enter_context(self.nc.semaphore("sem_" + e))
        self.dmadone_cnt = 0
        self.dmadone_pub = {}

    def raw(self, eng, fn):
        self.steps[eng].append(fn)

    def _wait(self, waiter, tok):
        key, val = tok
        if waiter == "sync" and key in ("pe", "dve", "act", "pool"):
            if self.dma_safe[key] < val:
                self.cnt[key] += 1
                sem0 = self.sems[key]
                self.raw(key, lambda e, s=sem0: e.drain().then_inc(s, 1))
                self.dma_safe[key] = self.cnt[key]
            val = self.dma_safe[key]
            tok = (key, val)
        if key == "dma" and waiter != "sync":
            total = self.dma_cnt
            if total not in self.dmadone_pub:
                self.dmadone_cnt += 1
                self.dmadone_pub[total] = self.dmadone_cnt
                ds, dd = self.sems["dma"], self.sems["dmadone"]
                if self.hwm.get(("sync", "dma"), 0) < total:
                    self.hwm[("sync", "dma")] = total
                    self.raw("sync", lambda e, s=ds, v=total: e.wait_ge(s, v))
                self.raw("sync", lambda e, s=dd: e.sem_inc(s, 1))
            key, val = "dmadone", self.dmadone_pub[total]
        if self.hwm.get((waiter, key), 0) >= val:
            return
        self.hwm[(waiter, key)] = val
        sem = self.sems[key]
        self.raw(waiter, lambda e, s=sem, v=val: e.wait_ge(s, v))

    def _hazards(self, eng, r, w):
        for b in r:
            if b.w_tok is not None:
                self._wait(eng, b.w_tok)
        for b in w:
            if b.w_tok is not None:
                self._wait(eng, b.w_tok)
            for rk, rv in b.readers.items():
                self._wait(eng, (rk, rv))

    def do(self, eng, fn, r=(), w=()):
        assert self.in_stage
        self._hazards(eng, r, w)
        self.raw(eng, fn)
        self.last_op_idx[eng] = len(self.steps[eng]) - 1
        self.emitted.add(eng)
        for b in r:
            self.touch.append((eng, b, "r"))
        for b in w:
            self.touch.append((eng, b, "w"))

    def dma(self, out_ap, in_ap, r=(), w=()):
        assert self.in_stage
        self._hazards("sync", r, w)
        self.dma_cnt += 16
        val = self.dma_cnt
        sem = self.sems["dma"]
        self.raw("sync",
                 lambda e, o=out_ap, i=in_ap, s=sem: e.dma_start(out=o, in_=i)
                 .then_inc(s, 16))
        for b in r:
            b.readers["dma"] = max(b.readers.get("dma", 0), val)
        for b in w:
            b.w_tok = ("dma", val)
            b.readers = {}

    @contextmanager
    def stage(self, use_drain=False, dma_out=False):
        use_drain = use_drain or dma_out
        assert not self.in_stage
        self.in_stage = True
        self.touch, self.emitted = [], set()
        self.last_op_idx = {}
        yield
        touched, emitted = self.touch, self.emitted
        self.in_stage = False
        self.touch = self.emitted = None
        toks = {}
        for e in emitted:
            self.cnt[e] += 1
            toks[e] = (e, self.cnt[e])
            sem = self.sems[e]
            if not use_drain:
                idx = self.last_op_idx[e]
                orig = self.steps[e][idx]
                self.steps[e][idx] = (
                    lambda en, o=orig, s=sem: o(en).then_inc(s, 1))
            else:
                self.raw(e, lambda en, s=sem: en.drain().then_inc(s, 1))
                self.dma_safe[e] = self.cnt[e]
        for eng, b, kind in touched:
            tok = toks[eng]
            if kind == "w":
                b.w_tok = tok
                b.readers = {}
            else:
                k, v = tok
                b.readers[k] = max(b.readers.get(k, 0), v)

    def barrier(self):
        with self.stage(use_drain=True):
            for e in ("pe", "dve", "act", "pool"):
                self.do(e, lambda en: en.nop())
        toks = [(e, self.cnt[e]) for e in ("pe", "dve", "act", "pool")]
        toks.append(("dma", self.dma_cnt))
        for e in ("pe", "dve", "act", "pool", "sync"):
            for t in toks:
                self._wait(e, t)

    def finalize(self, blk):
        for e, bname in ENG2BLK.items():
            fns = self.steps[e]
            getattr(blk, bname)(lambda eng, fns=fns: [f(eng) for f in fns])


def host_prep(inputs):
    f = lambda k: np.asarray(inputs[k])
    qkv = np.concatenate([f("L0_attn_q_w"), f("L0_attn_k_w"),
                          f("L0_attn_v_w")], axis=1)
    bw = f("L0_attn_b_w")

    def taps(name):
        return f(name)[:, 0, ::-1]
    wtab = np.concatenate([taps("L0_attn_qc_w"), taps("L0_attn_kc_w"),
                           taps("L0_attn_vc_w"), taps("L0_attn_fir_w1"),
                           taps("L0_attn_fir_w3"), taps("L0_attn_fir_w7"),
                           taps("L0_attn_fir_w31")], axis=1)
    sel = np.zeros((24, 24, 128), np.float32)
    for j in range(24):
        sel[j, j, :] = 1.0
    c = lambda a: np.ascontiguousarray(a)
    return dict(
        qkv=c(qkv.astype(np.float32)),
        bw=c(bw.astype(np.float32)),
        w1=c(f("L0_attn_mlp_w1").astype(np.float32)),
        w2=c(f("L0_attn_mlp_w2").astype(np.float32)),
        ow=c(f("L0_attn_o_w").astype(np.float32)),
        wtab=c(wtab.astype(np.float32)),
        wgu=c(np.concatenate([f("L0_ffn_w_gate").T, f("L0_ffn_w_up").T], axis=1)),
        wdT=c(f("L0_ffn_w_down").T),
        b1fm=c(f("L0_attn_mlp_b1").reshape(8, 128).T.astype(np.float32)),
        onorm2=c(f("L0_attn_onorm_w").reshape(2, 128).T.astype(np.float32)),
        b2row=c(f("L0_attn_mlp_b2").reshape(1, 24).astype(np.float32)),
        gltrow=c(f("L0_attn_gate_log_temp").reshape(1, 4).astype(np.float32)),
        id128=np.eye(128, dtype=np.float32),
        maskUs=c(np.triu(np.ones((128, 128), np.float32), 1)),
        maskUi=c(np.triu(np.ones((128, 128), np.float32), 0)),
        sel24=c(sel.reshape(24, 24 * 128)),
        ones1r=np.ones((1, 128), np.float32),
    )


class Ctx:
    pass


def build_program():
    nc = bass.Bass("TRN2", target_bir_lowering=False)
    C = Ctx()
    C.nc = nc
    d = lambda n, sh, dt, kind: nc.dram_tensor(n, sh, dt, kind=kind).ap()
    C.x_h = d("x", [L, D], F32, "ExternalInput")
    C.qkv_h = d("qkv", [D, 3072], F32, "ExternalInput")
    C.bw_h = d("bw", [D, 4], F32, "ExternalInput")
    C.w1_h = d("w1", [1120, D], F32, "ExternalInput")
    C.w2_h = d("w2", [D, 24], F32, "ExternalInput")
    C.ow_h = d("ow", [D, D], F32, "ExternalInput")
    C.wtab_h = d("wtab", [D, 54], F32, "ExternalInput")
    C.wgu_h = d("wgu", [D, 2 * HID], I32, "ExternalInput")
    C.wdT_h = d("wdT", [HID, D], I32, "ExternalInput")
    C.b1fm_h = d("b1fm", [128, 8], F32, "ExternalInput")
    C.onorm2_h = d("onorm2", [128, 2], F32, "ExternalInput")
    C.b2row_h = d("b2row", [1, 24], F32, "ExternalInput")
    C.gltrow_h = d("gltrow", [1, 4], F32, "ExternalInput")
    C.id128_h = d("id128", [128, 128], F32, "ExternalInput")
    C.maskUs_h = d("maskUs", [128, 128], F32, "ExternalInput")
    C.maskUi_h = d("maskUi", [128, 128], F32, "ExternalInput")
    C.sel24_h = d("sel24", [24, 24 * 128], F32, "ExternalInput")
    C.ones1r_h = d("ones1r", [1, 128], F32, "ExternalInput")
    C.out_h = d("out", [L, D], F32, "ExternalOutput")
    C.dbg = {}
    if "xn" in DEBUG:
        C.dbg["xn"] = nc.dram_tensor("dbg_xn", [NT, 128, D], BF16,
                                     kind="ExternalOutput").ap()
    for name, sh, dt_ in [("dd", [128, 4096], BF16), ("dd2", [128, 2048], BF16),
                          ("rst", [128, 16], F32),
                          ("q", [128, 4096], BF16), ("k", [128, 4096], BF16),
                          ("v", [128, 4096], BF16), ("beta", [128, 16], F32),
                          ("delta", [128, 1024], BF16), ("stats", [128, 96], F32),
                          ("wgt", [128, 24], F32), ("s", [128, 1024], F32),
                          ("hg", [128, 1024], F32), ("g", [128, 1024], F32)]:
        if name in DEBUG:
            C.dbg[name] = d("dbg_" + name, sh, dt_, "ExternalOutput")
    C.xn_s = d("xn_s", [NT, 128, D], BF16, "Internal")
    C.s_s = d("s_s", [NT, 128, D], F32, "Internal")
    C.diag_s = d("diag_s", [G, 128, 54 * 128], BF16, "Internal")
    C.ow_s = d("ow_s", [8, 128, D], BF16, "Internal")
    C.qkv_s = d("qkv_s", [12, 128, 2048], BF16, "Internal")
    C.wgu_s = d("wgu_s", [8, 128, 2 * HID], BF16, "Internal")
    C.wdT_s = d("wdT_s", [24, 128, D], BF16, "Internal")

    es = ExitStack()
    P = Prog(nc)
    P.make_sems(es)
    sbg = lambda n, sh, dt=F32: es.enter_context(nc.sbuf_tensor(n, sh, dt))

    C.idb = sbg("idb", [128, 128], BF16)
    C.mUsb = sbg("mUsb", [128, 128], BF16)
    C.mUib = sbg("mUib", [128, 128], BF16)
    C.i4b = sbg("i4b", [128, 512], BF16)
    C.onesb = sbg("onesb", [128, 1], BF16)
    C.onesrow = sbg("onesrow", [1, 128], BF16)
    C.rows = sbg("rows", [1, 96])
    C.tiny = sbg("tiny", [128, 32])
    C.invtempT = sbg("invtempT", [128, 24])
    C.b2i = sbg("b2i", [128, 24])
    C.b1fm = sbg("b1fm_s", [128, 8])
    C.wtab = sbg("wtab_s", [128, G * 54])
    C.w2b = sbg("w2b", [128, 8 * 24], BF16)
    C.wbb = sbg("wbb", [128, 8 * 4], BF16)
    C.sel24b = sbg("sel24b", [24, 24 * 128], BF16)
    C.constt = sbg("constt", [128, 8])
    CONSTS = [1e-5, 1e-6, 1e-30, 1.0]
    for i, v in enumerate(CONSTS):
        nc.const_aps.register(F32, v, C.constt[:, i:i + 1])

    C.PA = es.enter_context(nc.psum_tensor("PA", [128, 1024], F32))
    C.PB = es.enter_context(nc.psum_tensor("PB", [128, 1024], F32))
    C.PC = es.enter_context(nc.psum_tensor("PC", [128, 1024], F32))
    C.PS = es.enter_context(nc.psum_tensor("PS", [128, 512], F32))
    C.PT = es.enter_context(nc.psum_tensor("PT", [128, 1024], BF16))
    C.P = P

    with es, nc.Block() as blk:
        with P.stage():
            for i, v in enumerate(CONSTS):
                P.do("dve", lambda e, i=i, v=v: e.memset(C.constt[:, i:i + 1], v),
                     w=[P.res("constt")])
        P.barrier()
        with ExitStack() as es_pa:
            sba = lambda n, sh, dt=F32: es_pa.enter_context(nc.sbuf_tensor(n, sh, dt))
            C.w1b = sba("w1b", [128, 9 * 1024], BF16)
            C.owt = sba("owt", [128, 8 * 1024], BF16)
            with ExitStack() as es_p0:
                phase0(C, es_p0)
            P.barrier()
            with ExitStack() as es_a:
                phaseA(C, es_a)
        P.barrier()
        with ExitStack() as es_b:
            phaseB(C, es_b)
        P.raw("sync", lambda e, s=P.sems["dma"], v=P.dma_cnt: e.wait_ge(s, v))
        P.finalize(blk)
    return nc, list(C.dbg.keys())


def phase0(C, es):
    nc, P = C.nc, C.P
    R = P.res
    sb = lambda n, sh, dt=F32: es.enter_context(nc.sbuf_tensor(n, sh, dt))
    stg_f = [sb("p0f0", [128, 3072]), sb("p0f1", [128, 3072])]
    stg_i = [sb("p0i0", [128, 3072], I32), sb("p0i1", [128, 3072], I32)]
    stg_b = [sb("p0b0", [128, 3072], BF16), sb("p0b1", [128, 3072], BF16)]
    dgb = [sb("p0d0", [128, 54 * 128], BF16), sb("p0d1", [128, 54 * 128], BF16)]
    cons = sb("p0cons", [128, 512])
    on2 = sb("p0on2", [128, 2])
    xt = [sb("p0x0", [128, D]), sb("p0x1", [128, D])]
    xsq = [sb("p0sq0", [128, D]), sb("p0sq1", [128, D])]
    xnt = [sb("p0xnt0", [128, D], BF16), sb("p0xnt1", [128, D], BF16)]
    xnf = [sb("p0xnf0", [128, D], BF16), sb("p0xnf1", [128, D], BF16)]

    rf = lambda i: R("p0f%d" % i)
    ri = lambda i: R("p0i%d" % i)
    rb = lambda i: R("p0b%d" % i)

    with P.stage():
        P.dma(cons[:, 0:128], C.id128_h[:], w=[R("cons")])
        P.dma(cons[:, 128:256], C.maskUs_h[:], w=[R("cons")])
        P.dma(cons[:, 256:384], C.maskUi_h[:], w=[R("cons")])
        P.dma(C.rows[0:1, 0:24], C.b2row_h[:], w=[R("rows")])
        P.dma(C.rows[0:1, 24:28], C.gltrow_h[:], w=[R("rows")])
        P.dma(C.b1fm[:], C.b1fm_h[:], w=[R("b1fm")])
        P.dma(on2[:], C.onorm2_h[:], w=[R("on2")])
        P.dma(C.wtab[:].rearrange("p (g w) -> p g w", g=8),
              C.wtab_h.rearrange("(g p) w -> p g w", p=128), w=[R("wtab")])
        P.dma(stg_f[0][0:24, 0:3072], C.sel24_h[:], w=[rf(0)])
        P.dma(stg_f[1][0:1, 0:128], C.ones1r_h[:], w=[rf(1)])
    with P.stage():
        P.do("dve", lambda e: e.tensor_copy(out=C.idb[:], in_=cons[:, 0:128]),
             r=[R("cons")], w=[R("idb")])
        P.do("dve", lambda e: e.tensor_copy(out=C.mUsb[:], in_=cons[:, 128:256]),
             r=[R("cons")], w=[R("mUsb")])
        P.do("dve", lambda e: e.tensor_copy(out=C.mUib[:], in_=cons[:, 256:384]),
             r=[R("cons")], w=[R("mUib")])
        P.do("dve", lambda e: e.memset(C.onesb[:], 1.0), w=[R("onesb")])
        P.do("dve", lambda e: e.tensor_copy(out=C.onesrow[:],
             in_=stg_f[1][0:1, 0:128]), r=[rf(1)], w=[R("onesrow")])
        P.do("dve", lambda e: e.tensor_copy(out=C.sel24b[:],
             in_=stg_f[0][0:24, 0:3072]), r=[rf(0)], w=[R("sel24b")])
        P.do("pool", lambda e: e.tensor_copy(out=C.i4b[:, 0:128], in_=cons[:, 0:128]),
             r=[R("cons")], w=[R("i4b")])
        for j in range(1, 4):
            P.do("pool", lambda e, j=j: e.tensor_copy(
                out=C.i4b[:, j * 128:(j + 1) * 128], in_=cons[:, 0:128]),
                r=[R("cons")], w=[R("i4b")])

    ro = C.rows
    with P.stage():
        P.do("dve", lambda e: e.tensor_scalar(out=ro[0:1, 28:32], in0=ro[0:1, 24:28],
             scalar1=0.0, scalar2=None, op0=ALU.max), r=[R("rows")], w=[R("rows")])
        P.do("act", lambda e: e.activation(out=ro[0:1, 32:36], in_=ro[0:1, 24:28],
             func=AF.Abs), r=[R("rows")], w=[R("rows2")])
    with P.stage():
        P.do("act", lambda e: e.activation(out=ro[0:1, 32:36], in_=ro[0:1, 32:36],
             func=AF.Exp, scale=-1.0), r=[R("rows2")], w=[R("rows2")])
    with P.stage():
        P.do("act", lambda e: e.activation(out=ro[0:1, 32:36], in_=ro[0:1, 32:36],
             func=AF.Ln, bias=1.0), r=[R("rows2")], w=[R("rows2")])
    with P.stage():
        P.do("dve", lambda e: e.tensor_tensor(out=ro[0:1, 36:40], in0=ro[0:1, 28:32],
             in1=ro[0:1, 32:36], op=ALU.add), r=[R("rows"), R("rows2")], w=[R("rows")])
    with P.stage():
        P.do("dve", lambda e: e.tensor_scalar(out=ro[0:1, 36:40], in0=ro[0:1, 36:40],
             scalar1=0.5, scalar2=None, op0=ALU.add), r=[R("rows")], w=[R("rows")])
    with P.stage():
        P.do("dve", lambda e: e.reciprocal(out=ro[0:1, 40:44], in_=ro[0:1, 36:40]),
             r=[R("rows")], w=[R("rows")])
    with P.stage():
        P.do("dve", lambda e: e.tensor_copy(
            out=ro[0:1, 44:68].rearrange("p (h s) -> p h s", h=4),
            in_=ro[0:1, 40:44].unsqueeze(2).broadcast_to([1, 4, 6])),
            r=[R("rows")], w=[R("rows")])
    row_b = es.enter_context(nc.sbuf_tensor("p0rowb", [1, 64], BF16))
    with P.stage():
        P.do("dve", lambda e: e.tensor_copy(out=row_b[0:1, 0:24],
             in_=ro[0:1, 44:68]), r=[R("rows")], w=[R("rowb")])
        P.do("dve", lambda e: e.tensor_copy(out=row_b[0:1, 32:56],
             in_=ro[0:1, 0:24]), r=[R("rows")], w=[R("rowb")])
    with P.stage():
        P.do("pe", lambda e: e.matmul(C.PS[:, 0:24], lhsT=C.onesrow[:],
             rhs=row_b[0:1, 0:24], start=True, stop=True),
             r=[R("rowb"), R("onesrow")], w=[R("PS")])
        P.do("pe", lambda e: e.matmul(C.PS[:, 32:56], lhsT=C.onesrow[:],
             rhs=row_b[0:1, 32:56], start=True, stop=True),
             r=[R("rowb"), R("onesrow")], w=[R("PS")])
    with P.stage():
        P.do("dve", lambda e: e.tensor_copy(out=C.invtempT[:], in_=C.PS[:, 0:24]),
             r=[R("PS")], w=[R("invtempT")])
    with P.stage():
        P.do("dve", lambda e: e.tensor_tensor(out=C.b2i[:], in0=C.PS[:, 32:56],
             in1=C.invtempT[:], op=ALU.mult),
             r=[R("PS"), R("invtempT")], w=[R("b2i")])

    for k in range(8):
        p = k % 2
        with P.stage():
            P.dma(stg_f[p][:], C.qkv_h[k * 128:(k + 1) * 128, :], w=[rf(p)])
        with P.stage(dma_out=True):
            P.do("dve", lambda e, p=p: e.tensor_copy(out=stg_b[p][:],
                 in_=stg_f[p][:]), r=[rf(p)], w=[rb(p)])
        with P.stage():
            P.dma(C.qkv_s[:, :, k * 256:(k + 1) * 256].transpose([1, 0, 2]),
                  stg_b[p][:].rearrange("p (m n) -> p m n", m=12), r=[rb(p)])
    with P.stage():
        P.dma(stg_f[0][:, 0:32].rearrange("p (k n) -> p k n", k=8),
              C.bw_h.rearrange("(k p) n -> p k n", p=128), w=[rf(0)])
    with P.stage():
        P.do("dve", lambda e: e.tensor_copy(out=C.wbb[:], in_=stg_f[0][:, 0:32]),
             r=[rf(0)], w=[R("wbb")])
    for k in range(9):
        rows_n = 128 if k < 8 else 96
        p = k % 2
        with P.stage():
            P.dma(stg_f[p][0:rows_n, 0:1024], C.w1_h[k * 128:k * 128 + rows_n, :],
                  w=[rf(p)])
        with P.stage():
            P.do("dve", lambda e, k=k, rows_n=rows_n, p=p: e.tensor_copy(
                out=C.w1b[0:rows_n, k * 1024:(k + 1) * 1024],
                in_=stg_f[p][0:rows_n, 0:1024]), r=[rf(p)], w=[R("w1b")])
    with P.stage():
        P.dma(stg_f[0][:, 0:192].rearrange("p (c n) -> p c n", c=8),
              C.w2_h.rearrange("(c p) n -> p c n", p=128), w=[rf(0)])
    with P.stage():
        P.do("dve", lambda e: e.tensor_copy(out=C.w2b[:], in_=stg_f[0][:, 0:192]),
             r=[rf(0)], w=[R("w2b")])
    for k in range(8):
        p = k % 2
        with P.stage():
            P.dma(stg_f[p][:, 0:1024], C.ow_h[k * 128:(k + 1) * 128, :], w=[rf(p)])
        with P.stage(dma_out=True):
            P.do("dve", lambda e, p=p, k=k: e.tensor_scalar(
                out=stg_b[p][:, 0:1024], in0=stg_f[p][:, 0:1024],
                scalar1=on2[:, k % 2:k % 2 + 1], scalar2=None, op0=ALU.mult),
                r=[rf(p), R("on2")], w=[rb(p)])
        with P.stage():
            P.dma(C.ow_s[k], stg_b[p][:, 0:1024], r=[rb(p)])
    for k in range(8):
        for half in range(2):
            p = (2 * k + half) % 2
            with P.stage():
                P.dma(stg_i[p][:], C.wgu_h[k * 128:(k + 1) * 128,
                                           half * HID:(half + 1) * HID], w=[ri(p)])
            with P.stage(dma_out=True):
                P.do("dve", lambda e, p=p: e.tensor_copy(out=stg_b[p][:],
                     in_=stg_i[p][:]), r=[ri(p)], w=[rb(p)])
            with P.stage():
                P.dma(C.wgu_s[k][:, half * HID:(half + 1) * HID], stg_b[p][:],
                      r=[rb(p)])
    for c in range(24):
        p = c % 2
        with P.stage():
            P.dma(stg_i[p][:, 0:1024], C.wdT_h[c * 128:(c + 1) * 128, :], w=[ri(p)])
        with P.stage(dma_out=True):
            P.do("dve", lambda e, p=p: e.tensor_copy(out=stg_b[p][:, 0:1024],
                 in_=stg_i[p][:, 0:1024]), r=[ri(p)], w=[rb(p)])
        with P.stage():
            P.dma(C.wdT_s[c], stg_b[p][:, 0:1024], r=[rb(p)])
    for gg in range(G):
        p = gg % 2
        with P.stage(dma_out=True):
            for j in range(54):
                P.do("dve", lambda e, p=p, gg=gg, j=j: e.tensor_scalar(
                    out=dgb[p][:, j * 128:(j + 1) * 128], in0=C.idb[:],
                    scalar1=C.wtab[:, gg * 54 + j:gg * 54 + j + 1], scalar2=None,
                    op0=ALU.mult), r=[R("idb"), R("wtab")], w=[R("dg%d" % p)])
        with P.stage():
            P.dma(C.diag_s[gg], dgb[p][:], r=[R("dg%d" % p)])

    T = C.tiny
    for t in range(NT):
        p = t % 2
        xr, sqr = R("p0x%d" % p), R("p0sq%d" % p)
        ntr, nfr = R("p0xnt%d" % p), R("p0xnf%d" % p)
        ln = R("p0ln%d" % p)
        co = p * 16
        with P.stage():
            P.dma(xt[p][:], C.x_h[t * 128:(t + 1) * 128, :], w=[xr])
        with P.stage():
            P.do("act", lambda e, p=p: e.activation(out=xsq[p][:], in_=xt[p][:],
                 func=AF.Square), r=[xr], w=[sqr])
            P.do("dve", lambda e, p=p, co=co: e.reduce_sum(out=T[:, co:co + 1],
                 in_=xt[p][:], axis=AX.XYZW), r=[xr], w=[ln])
        with P.stage():
            P.do("dve", lambda e, p=p, co=co: e.reduce_sum(out=T[:, co + 1:co + 2],
                 in_=xsq[p][:], axis=AX.XYZW), r=[sqr, ln], w=[ln])
            P.do("dve", lambda e, co=co: e.tensor_scalar(out=T[:, co + 2:co + 3],
                 in0=T[:, co:co + 1], scalar1=1.0 / D, scalar2=None, op0=ALU.mult),
                 r=[ln], w=[ln])
        with P.stage():
            P.do("dve", lambda e, co=co: e.tensor_tensor(out=T[:, co + 3:co + 4],
                 in0=T[:, co + 2:co + 3], in1=T[:, co + 2:co + 3], op=ALU.mult),
                 r=[ln], w=[ln])
        with P.stage():
            P.do("dve", lambda e, co=co: e.scalar_tensor_tensor(
                 out=T[:, co + 4:co + 5], in0=T[:, co + 1:co + 2], scalar=1.0 / D,
                 in1=T[:, co + 3:co + 4], op0=ALU.mult, op1=ALU.subtract),
                 r=[ln], w=[ln])
        with P.stage():
            P.do("act", lambda e, co=co: e.activation(out=T[:, co + 5:co + 6],
                 in_=T[:, co + 4:co + 5], func=AF.Ln, bias=1e-5), r=[ln], w=[ln])
        with P.stage():
            P.do("act", lambda e, co=co: e.activation(out=T[:, co + 6:co + 7],
                 in_=T[:, co + 5:co + 6], func=AF.Exp, scale=-0.5), r=[ln], w=[ln])
        with P.stage():
            P.do("dve", lambda e, p=p, co=co: e.tensor_scalar(out=xsq[p][:],
                 in0=xt[p][:], scalar1=T[:, co + 2:co + 3], scalar2=None,
                 op0=ALU.subtract), r=[xr, ln], w=[sqr])
        with P.stage():
            P.do("dve", lambda e, p=p, co=co: e.tensor_scalar(out=xnt[p][:],
                 in0=xsq[p][:], scalar1=T[:, co + 6:co + 7], scalar2=None,
                 op0=ALU.mult), r=[sqr, ln], w=[ntr])
        with P.stage():
            for cch in range(8):
                P.do("pe", lambda e, p=p, cch=cch: e.transpose(
                    C.PT[:, cch * 128:(cch + 1) * 128],
                    in_=xnt[p][:, cch * 128:(cch + 1) * 128], identity=C.idb[:]),
                    r=[ntr, R("idb")], w=[R("PT")])
        with P.stage(dma_out=True):
            P.do("dve", lambda e, p=p: e.tensor_copy(out=xnf[p][:],
                 in_=C.PT[:, 0:1024]), r=[R("PT")], w=[nfr])
        with P.stage():
            P.dma(C.xn_s[t], xnf[p][:], r=[nfr])
            if "xn" in C.dbg:
                P.dma(C.dbg["xn"][t], xnf[p][:], r=[nfr])


def phaseA(C, es):
    nc, P = C.nc, C.P
    R = P.res
    sb = lambda n, sh, dt=F32: es.enter_context(nc.sbuf_tensor(n, sh, dt))
    xnst = sb("a_xnst", [128, 4096], BF16)
    qpre = sb("a_qpre", [128, 8 * 516], BF16)
    kpre = sb("a_kpre", [128, 8 * 516], BF16)
    vpre = sb("a_vpre", [128, 8 * 516], BF16)
    vf = sb("a_vf", [128, 8 * 544], BF16)
    f3 = sb("a_f3", [128, 4096], BF16)
    f7 = sb("a_f7", [128, 4096], BF16)
    f31 = sb("a_f31", [128, 4096], BF16)
    wpair = [sb("a_wp0", [128, 2048], BF16), sb("a_wp1", [128, 2048], BF16)]
    dg = [sb("a_dg0", [128, 54 * 128], BF16), sb("a_dg1", [128, 54 * 128], BF16)]
    betaf = sb("a_betaf", [4, 512], BF16)
    betat = sb("a_betat", [128, 16])
    rst = sb("a_rst", [128, 16])
    rr2 = sb("a_rr2", [128, 8])
    sq2 = sb("a_sq2", [128, 1024], BF16)
    kntm = sb("a_kntm", [128, 1024], BF16)
    kbtm = sb("a_kbtm", [128, 1024], BF16)
    vbtm = sb("a_vbtm", [128, 1024], BF16)
    Apre = sb("a_apre", [128, 512], BF16)
    ATs = sb("a_ats0", [128, 512], BF16)
    As = Apre
    Xp = [sb("a_xp0", [128, 512], BF16), sb("a_xp1", [128, 512], BF16)]
    XTp = [sb("a_xtp0", [128, 512], BF16), sb("a_xtp1", [128, 512], BF16)]
    PTs = sb("a_pts", [128, 512], BF16)
    tmp_tm = sb("a_tmptm", [128, 1024], BF16)
    wnf = sb("a_wnf", [128, 1024], BF16)
    uhat = sb("a_uhat", [128, 1024], BF16)
    S = sb("a_S", [128, 2048])
    Sbb = sb("a_Sbb", [128, 2048], BF16)
    dfm2 = sb("a_dfm2", [128, 2048], BF16)
    b1f2 = sb("a_b1f2", [128, 2048], BF16)
    attb = sb("a_attb", [128, 512], BF16)
    babs = sb("a_babs", [128, 1024], BF16)
    bsq = sb("a_bsq", [128, 1024], BF16)
    st96 = sb("a_st96", [128, 96], BF16)
    stT = sb("a_stT", [96, 128], BF16)
    l24 = sb("a_l24", [128, 24])
    w24 = sb("a_w24", [128, 24], BF16)
    wgtT = sb("a_wgtT", [24, 128], BF16)
    hgst = sb("a_hgst", [128, 2048], BF16)
    facc = sb("a_facc", [128, 1024], BF16)
    tmpc = sb("a_tmpc", [128, 1024], BF16)
    ffm = sb("a_ffm", [128, 1024], BF16)
    sot = sb("a_sot", [128, 1024])
    rrow = sb("a_rrow", [1, 512], BF16)
    PA, PB, PC, PS, PT = C.PA, C.PB, C.PC, C.PS, C.PT

    def half(pt, i):
        return pt[:, i * 512:(i + 1) * 512]

    with P.stage():
        P.dma(C.owt[:].rearrange("p (k n) -> p k n", k=8),
              C.ow_s.rearrange("k p n -> p k n"), w=[R("owt")])
        P.do("dve", lambda e: e.memset(S[:], 0.0), w=[R("a_S")])
        P.do("pool", lambda e: e.memset(Sbb[:], 0.0), w=[R("a_Sbb")])
        P.do("pool", lambda e: e.memset(qpre[:], 0.0), w=[R("a_qpre")])
        P.do("pool", lambda e: e.memset(kpre[:], 0.0), w=[R("a_kpre")])
        P.do("pool", lambda e: e.memset(vpre[:], 0.0), w=[R("a_vpre")])
        P.do("dve", lambda e: e.memset(vf[:], 0.0), w=[R("a_vf")])

    for st in range(NST):
        t0 = st * 4
        with P.stage():
            for i in range(4):
                P.dma(xnst[:].rearrange("p (g t) -> p g t", g=8)
                      [:, :, i * 128:(i + 1) * 128],
                      C.xn_s[t0 + i].rearrange("p (g t) -> p g t", g=8),
                      w=[R("a_xnst")])
            P.dma(dg[0][:], C.diag_s[0], w=[R("a_dg0")])
            P.dma(dg[1][:], C.diag_s[1], w=[R("a_dg1")])
            P.dma(wpair[0][:], C.qkv_s[0], w=[R("a_wp0")])
            P.dma(wpair[1][:], C.qkv_s[1], w=[R("a_wp1")])
        dsts = [(qpre, "a_qpre"), (kpre, "a_kpre"), (vpre, "a_vpre")]
        for pair in range(12):
            pw = pair % 2
            Tn = pair // 4
            dstbuf, dstres = dsts[Tn]
            PP, ppres = (PA, ("PA0", "PA1")) if pair % 2 == 0 else (PB, ("PB0", "PB1"))
            with P.stage():
                for m2 in range(2):
                    for k in range(8):
                        P.do("pe", lambda e, pw=pw, m2=m2, k=k, PP=PP: e.matmul(
                            half(PP, m2),
                            lhsT=wpair[pw][:, k * 256 + m2 * 128:k * 256 + m2 * 128 + 128],
                            rhs=xnst[:, k * 512:(k + 1) * 512],
                            start=(k == 0), stop=(k == 7)),
                            r=[R("a_wp%d" % pw), R("a_xnst")],
                            w=[R(ppres[m2])])
            with P.stage():
                if pair + 2 < 12:
                    P.dma(wpair[pw][:], C.qkv_s[pair + 2], w=[R("a_wp%d" % pw)])
                for m2 in range(2):
                    gout = (pair % 4) * 2 + m2
                    eng = "dve" if m2 == 0 else "act"
                    if eng == "dve":
                        P.do("dve", lambda e, PP=PP, m2=m2, dstbuf=dstbuf, gout=gout:
                             e.tensor_copy(out=dstbuf[:, gout * 516 + 4:gout * 516 + 516],
                                           in_=half(PP, m2)),
                             r=[R(ppres[m2])], w=[R(dstres)])
                    else:
                        P.do("act", lambda e, PP=PP, m2=m2, dstbuf=dstbuf, gout=gout:
                             e.activation(out=dstbuf[:, gout * 516 + 4:gout * 516 + 516],
                                          in_=half(PP, m2), func=AF.Identity),
                             r=[R(ppres[m2])], w=[R(dstres)])
        with P.stage():
            for k in range(8):
                P.do("pe", lambda e, k=k: e.matmul(
                    PS[0:4, 0:512], lhsT=C.wbb[:, k * 4:(k + 1) * 4],
                    rhs=xnst[:, k * 512:(k + 1) * 512],
                    start=(k == 0), stop=(k == 7)),
                    r=[R("wbb"), R("a_xnst")], w=[R("PS")])
        with P.stage():
            P.do("act", lambda e: e.activation(out=betaf[:], in_=PS[0:4, 0:512],
                 func=AF.Sigmoid), r=[R("PS")], w=[R("a_betaf")])
        with P.stage():
            for i in range(4):
                P.do("pe", lambda e, i=i: e.transpose(
                    PT[:, i * 4:(i + 1) * 4],
                    in_=betaf[0:4, i * 128:(i + 1) * 128], identity=C.idb[0:4, 0:4]),
                    r=[R("a_betaf"), R("idb")], w=[R("PT")])
        with P.stage():
            P.do("dve", lambda e: e.tensor_copy(out=betat[:], in_=PT[:, 0:16]),
                 r=[R("PT")], w=[R("a_betat")])
            if "beta" in C.dbg and st == 0:
                P.do("dve", lambda e: e.tensor_copy(out=l24[:, 0:16], in_=PT[:, 0:16]),
                     r=[R("PT")], w=[R("a_l24")])
        if "beta" in C.dbg and st == 0:
            with P.stage():
                P.dma(C.dbg["beta"][:, 0:16], l24[:, 0:16], r=[R("a_l24")])
        CONV = [("q", qpre, "a_qpre", 0, 4), ("k", kpre, "a_kpre", 4, 4),
                ("v", vpre, "a_vpre", 8, 4)]
        pcrot = [("PA0", lambda: half(PA, 0)), ("PA1", lambda: half(PA, 1)),
                 ("PB0", lambda: half(PB, 0)), ("PB1", lambda: half(PB, 1)),
                 ("PC0", lambda: half(PC, 0)), ("PC1", lambda: half(PC, 1))]
        pcidx = 0
        for g in range(8):
            dgr = R("a_dg%d" % (g % 2))
            dgb_ = dg[g % 2]
            for name, buf, bres, tbase, W in CONV:
                pres, pap = pcrot[pcidx % 6]
                pcidx += 1
                with P.stage():
                    for j in range(W):
                        P.do("pe", lambda e, dgb_=dgb_, tbase=tbase, j=j, buf=buf,
                             g=g, pap=pap, W=W: e.matmul(
                                 pap(), lhsT=dgb_[:, (tbase + j) * 128:(tbase + j + 1) * 128],
                                 rhs=buf[:, g * 516 + 4 - j:g * 516 + 4 - j + 512],
                                 start=(j == 0), stop=(j == W - 1)),
                            r=[dgr, R(bres)], w=[R(pres)])
                with P.stage():
                    P.do("dve", lambda e, buf=buf, g=g: e.tensor_copy(
                        out=buf[:, g * 516 + 1:g * 516 + 4],
                        in_=buf[:, g * 516 + 513:g * 516 + 516]),
                        r=[R(bres)], w=[R(bres)])
                with P.stage():
                    if name == "v":
                        P.do("act", lambda e, pap=pap, g=g: e.activation(
                            out=vf[:, g * 544 + 32:g * 544 + 544],
                            in_=pap(), func=AF.Silu),
                            r=[R(pres)], w=[R("a_vf")])
                    else:
                        P.do("act", lambda e, pap=pap, buf=buf, g=g: e.activation(
                            out=buf[:, g * 516 + 4:g * 516 + 516],
                            in_=pap(), func=AF.Silu),
                            r=[R(pres)], w=[R(bres)])
            with P.stage():
                P.do("dve", lambda e, g=g: e.tensor_copy(
                    out=vf[:, g * 544 + 1:g * 544 + 32],
                    in_=vf[:, g * 544 + 513:g * 544 + 544]),
                    r=[R("a_vf")], w=[R("a_vf")])
            firs = [(13, 3, f3, "a_f3"), (16, 7, f7, "a_f7"), (23, 31, f31, "a_f31")]
            fp = []
            for tbase, W, fbuf, fres in firs:
                pres, pap = pcrot[pcidx % 6]
                pcidx += 1
                fp.append((pres, pap, fbuf, fres))
                with P.stage():
                    for j in range(W):
                        P.do("pe", lambda e, dgb_=dgb_, tbase=tbase, j=j, g=g,
                             pap=pap, W=W: e.matmul(
                                 pap(), lhsT=dgb_[:, (tbase + j) * 128:(tbase + j + 1) * 128],
                                 rhs=vf[:, g * 544 + 32 - j:g * 544 + 32 - j + 512],
                                 start=(j == 0), stop=(j == W - 1)),
                            r=[dgr, R("a_vf")], w=[R(pres)])
            with P.stage():
                for ii, (pres, pap, fbuf, fres) in enumerate(fp):
                    eng = ["dve", "act", "dve"][ii]
                    if eng == "dve":
                        P.do("dve", lambda e, pap=pap, fbuf=fbuf, g=g: e.tensor_copy(
                            out=fbuf[:, g * 512:(g + 1) * 512], in_=pap()),
                            r=[R(pres)], w=[R(fres)])
                    else:
                        P.do("act", lambda e, pap=pap, fbuf=fbuf, g=g: e.activation(
                            out=fbuf[:, g * 512:(g + 1) * 512], in_=pap(),
                            func=AF.Identity), r=[R(pres)], w=[R(fres)])
            if g + 2 < 8:
                with P.stage():
                    P.dma(dg[g % 2][:], C.diag_s[g + 2], w=[R("a_dg%d" % (g % 2))])
        if "q" in C.dbg and st == 0:
            with P.stage():
                P.dma(C.dbg["q"][:], qpre[:, 0:4096], r=[R("a_qpre")])
                P.dma(C.dbg["k"][:], kpre[:, 0:4096], r=[R("a_kpre")])
                P.dma(C.dbg["v"][:], vf[:, 0:4096], r=[R("a_vf")])

        for ipair in range(2):
            for i in (2 * ipair, 2 * ipair + 1):
                t = t0 + i
                o = i * 128
                qs = lambda ch, o=o: qpre[:, ch * 516 + 4 + o:ch * 516 + 4 + o + 128]
                ks = lambda ch, o=o: kpre[:, ch * 516 + 4 + o:ch * 516 + 4 + o + 128]
                vs = lambda ch, o=o: vf[:, ch * 544 + 32 + o:ch * 544 + 32 + o + 128]
                bcol = lambda hh, i=i: betat[:, i * 4 + hh:i * 4 + hh + 1]
                tileA_phase1(C, P, R, locals())
            with P.stage():
                P.do("act", lambda e: e.activation(out=hgst[:], in_=hgst[:],
                     func=AF.Gelu), r=[R("a_hgst")], w=[R("a_hgst")])
            for i in (2 * ipair, 2 * ipair + 1):
                t = t0 + i
                o = i * 128
                tileA_phase2(C, P, R, locals())


def tileA_phase1(C, P, R, env):
    nc = C.nc
    g = env
    st, i, t, o = g["st"], g["i"], g["t"], g["o"]
    qs, ks, vs, bcol = g["qs"], g["ks"], g["vs"], g["bcol"]
    PA, PB, PC, PS, PT = C.PA, C.PB, C.PC, C.PS, C.PT
    half = g["half"]
    sq2, rst, kntm, kbtm, vbtm = g["sq2"], g["rst"], g["kntm"], g["kbtm"], g["vbtm"]
    Apre, ATs, As, Xp, XTp, PTs = g["Apre"], g["ATs"], g["As"], g["Xp"], g["XTp"], g["PTs"]
    tmp_tm, wnf, uhat = g["tmp_tm"], g["wnf"], g["uhat"]
    S, Sbb, dfm2, b1f2, attb = g["S"], g["Sbb"], g["dfm2"], g["b1f2"], g["attb"]
    babs, bsq, st96, stT = g["babs"], g["bsq"], g["st96"], g["stT"]
    hgst, xnst = g["hgst"], g["xnst"]
    f3, f7, f31, vf = g["f3"], g["f7"], g["f31"], g["vf"]
    sl = i % 2
    dfm = dfm2[:, sl * 1024:(sl + 1) * 1024]
    b1f = b1f2[:, sl * 1024:(sl + 1) * 1024]

    qstack = g["qpre"][:].rearrange("p (c n) -> p c n", c=8)[:, :, 4 + o:4 + o + 128]
    kstack = g["kpre"][:].rearrange("p (c n) -> p c n", c=8)[:, :, 4 + o:4 + o + 128]
    vstack = g["vf"][:].rearrange("p (c n) -> p c n", c=8)[:, :, 32 + o:32 + o + 128]
    for which, stk, sres, colbase in (("q", qstack, "a_qpre", 0),
                                      ("k", kstack, "a_kpre", 4)):
        with P.stage():
            P.do("dve", lambda e, stk=stk: e.tensor_tensor(
                out=sq2[:].rearrange("p (c n) -> p c n", c=8),
                in0=stk, in1=stk, op=ALU.mult),
                r=[R(sres)], w=[R("a_sq2")])
        with P.stage():
            for hh in range(4):
                for cc in range(2):
                    P.do("pe", lambda e, hh=hh, cc=cc, colbase=colbase: e.matmul(
                        PS[:, colbase + hh:colbase + hh + 1],
                        lhsT=sq2[:, (2 * hh + cc) * 128:(2 * hh + cc + 1) * 128],
                        rhs=C.onesb[:], start=(cc == 0), stop=(cc == 1)),
                        r=[R("a_sq2"), R("onesb")], w=[R("PS")])
    with P.stage():
        P.do("act", lambda e: e.activation(out=rst[:, 0:8], in_=PS[:, 0:8],
             func=AF.Ln, bias=1e-6), r=[R("PS")], w=[R("a_rst")])
    with P.stage():
        P.do("act", lambda e: e.activation(out=rst[:, 0:8], in_=rst[:, 0:8],
             func=AF.Exp, scale=-0.5), r=[R("a_rst")], w=[R("a_rst")])
    with P.stage():
        P.do("dve", lambda e: e.tensor_tensor(
            out=rst[:, 8:12], in0=rst[:, 4:8],
            in1=g["betat"][:, i * 4:(i + 1) * 4], op=ALU.mult),
            r=[R("a_rst"), R("a_betat")], w=[R("a_rst")])
    with P.stage():
        for ch in range(8):
            P.do("pe", lambda e, ch=ch: e.transpose(
                PT[:, ch * 128:(ch + 1) * 128], in_=ks(ch), identity=C.idb[:]),
                r=[R("a_kpre"), R("idb")], w=[R("PT")])
    with P.stage():
        for hh in range(4):
            P.do("dve", lambda e, hh=hh: e.tensor_scalar(
                out=kntm[:, hh * 256:(hh + 1) * 256],
                in0=PT[:, hh * 256:(hh + 1) * 256],
                scalar1=rst[:, 4 + hh:5 + hh], scalar2=None, op0=ALU.mult),
                r=[R("PT"), R("a_rst")], w=[R("a_kntm")])
            P.do("dve", lambda e, hh=hh: e.tensor_scalar(
                out=kbtm[:, hh * 256:(hh + 1) * 256],
                in0=PT[:, hh * 256:(hh + 1) * 256],
                scalar1=rst[:, 8 + hh:9 + hh], scalar2=None, op0=ALU.mult),
                r=[R("PT"), R("a_rst")], w=[R("a_kbtm")])
    with P.stage():
        for ch in range(8):
            P.do("pe", lambda e, ch=ch: e.transpose(
                PT[:, ch * 128:(ch + 1) * 128], in_=vs(ch), identity=C.idb[:]),
                r=[R("a_vf"), R("idb")], w=[R("PT")])
    with P.stage():
        for hh in range(4):
            P.do("dve", lambda e, hh=hh: e.tensor_scalar(
                out=vbtm[:, hh * 256:(hh + 1) * 256],
                in0=PT[:, hh * 256:(hh + 1) * 256],
                scalar1=bcol(hh), scalar2=None, op0=ALU.mult),
                r=[R("PT"), R("a_betat")], w=[R("a_vbtm")])
    with P.stage():
        for hh in range(4):
            for cc in range(2):
                P.do("pe", lambda e, hh=hh, cc=cc: e.matmul(
                    half(PC, 0)[:, hh * 128:(hh + 1) * 128],
                    lhsT=ks(2 * hh + cc), rhs=ks(2 * hh + cc),
                    start=(cc == 0), stop=(cc == 1)),
                    r=[R("a_kpre")], w=[R("PC0")])
    with P.stage():
        for hh in range(4):
            P.do("dve", lambda e, hh=hh: e.tensor_scalar(
                out=Apre[:, hh * 128:(hh + 1) * 128],
                in0=half(PC, 0)[:, hh * 128:(hh + 1) * 128],
                scalar1=rst[:, 8 + hh:9 + hh], scalar2=-1.0,
                op0=ALU.mult, op1=ALU.mult),
                r=[R("PC0"), R("a_rst")], w=[R("a_apre")])
    with P.stage():
        for hh in range(4):
            P.do("pe", lambda e, hh=hh: e.transpose(
                PT[:, hh * 128:(hh + 1) * 128],
                in_=Apre[:, hh * 128:(hh + 1) * 128], identity=C.idb[:]),
                r=[R("a_apre"), R("idb")], w=[R("PT")])
    with P.stage():
        for hh in range(4):
            P.do("dve", lambda e, hh=hh: e.scalar_tensor_tensor(
                out=ATs[:, hh * 128:(hh + 1) * 128],
                in0=PT[:, hh * 128:(hh + 1) * 128],
                scalar=rst[:, 4 + hh:5 + hh], in1=C.mUsb[:],
                op0=ALU.mult, op1=ALU.mult),
                r=[R("PT"), R("a_rst"), R("mUsb")], w=[R("a_ats0")])
    with P.stage():
        for hh in range(4):
            P.do("pe", lambda e, hh=hh: e.transpose(
                PT[:, hh * 128:(hh + 1) * 128],
                in_=ATs[:, hh * 128:(hh + 1) * 128], identity=C.idb[:]),
                r=[R("a_ats0"), R("idb")], w=[R("PT")])
    with P.stage():
        P.do("dve", lambda e: e.tensor_copy(out=As[:], in_=PT[:, 0:512]),
             r=[R("PT")], w=[R("a_apre")])
        P.do("dve", lambda e: e.tensor_tensor(out=PTs[:], in0=ATs[:],
             in1=C.i4b[:], op=ALU.add), r=[R("a_ats0"), R("i4b")], w=[R("a_pts")])
    xcur, xtcur = As, ATs
    rxc, rxtc = "a_apre", "a_ats0"
    for lvl_i, lvl in enumerate((2, 4, 8, 16, 32, 64)):
        need_xt = lvl < 64
        xn_, xtn = Xp[lvl_i % 2], XTp[lvl_i % 2]
        rxn, rxtn = "a_xp%d" % (lvl_i % 2), "a_xtp%d" % (lvl_i % 2)
        with P.stage():
            for hh in range(4):
                P.do("pe", lambda e, hh=hh, xc=xcur, xtc=xtcur: e.matmul(
                    half(PC, 0)[:, hh * 128:(hh + 1) * 128],
                    lhsT=xtc[:, hh * 128:(hh + 1) * 128],
                    rhs=xc[:, hh * 128:(hh + 1) * 128], start=True, stop=True),
                    r=[R(rxc), R(rxtc)], w=[R("PC0")])
            if need_xt:
                for hh in range(4):
                    P.do("pe", lambda e, hh=hh, xc=xcur, xtc=xtcur: e.matmul(
                        half(PC, 1)[:, hh * 128:(hh + 1) * 128],
                        lhsT=xc[:, hh * 128:(hh + 1) * 128],
                        rhs=xtc[:, hh * 128:(hh + 1) * 128], start=True, stop=True),
                        r=[R(rxc), R(rxtc)], w=[R("PC1")])
        with P.stage():
            P.do("dve", lambda e, xn_=xn_: e.tensor_copy(out=xn_[:], in_=half(PC, 0)),
                 r=[R("PC0")], w=[R(rxn)])
            if need_xt:
                P.do("act", lambda e, xtn=xtn: e.activation(out=xtn[:],
                     in_=half(PC, 1), func=AF.Identity), r=[R("PC1")], w=[R(rxtn)])
        with P.stage():
            for hh in range(4):
                P.do("pe", lambda e, hh=hh, xn_=xn_: e.matmul(
                    half(PB, 0)[:, hh * 128:(hh + 1) * 128],
                    lhsT=xn_[:, hh * 128:(hh + 1) * 128],
                    rhs=PTs[:, hh * 128:(hh + 1) * 128], start=True, stop=True),
                    r=[R(rxn), R("a_pts")], w=[R("PB0")])
        with P.stage():
            P.do("dve", lambda e: e.tensor_tensor(out=PTs[:], in0=half(PB, 0),
                 in1=PTs[:], op=ALU.add), r=[R("PB0"), R("a_pts")], w=[R("a_pts")])
        xcur, xtcur, rxc, rxtc = xn_, xtn, rxn, rxtn
    with P.stage():
        for hh in range(4):
            P.do("pe", lambda e, hh=hh: e.matmul(
                PB[:, hh * 256:(hh + 1) * 256],
                lhsT=PTs[:, hh * 128:(hh + 1) * 128],
                rhs=kbtm[:, hh * 256:(hh + 1) * 256], start=True, stop=True),
                r=[R("a_pts"), R("a_kbtm")], w=[R("PB0"), R("PB1")])
    with P.stage():
        P.do("act", lambda e: e.activation(out=tmp_tm[:], in_=PB[:, 0:1024],
             func=AF.Identity, scale=-1.0), r=[R("PB0"), R("PB1")], w=[R("a_tmptm")])
    with P.stage():
        for ch in range(8):
            P.do("pe", lambda e, ch=ch: e.transpose(
                PT[:, ch * 128:(ch + 1) * 128],
                in_=tmp_tm[:, ch * 128:(ch + 1) * 128], identity=C.idb[:]),
                r=[R("a_tmptm"), R("idb")], w=[R("PT")])
    with P.stage():
        P.do("dve", lambda e: e.tensor_copy(out=wnf[:], in_=PT[:, 0:1024]),
             r=[R("PT")], w=[R("a_wnf")])
    with P.stage():
        for hh in range(4):
            P.do("pe", lambda e, hh=hh: e.matmul(
                PC[:, hh * 256:(hh + 1) * 256],
                lhsT=PTs[:, hh * 128:(hh + 1) * 128],
                rhs=vbtm[:, hh * 256:(hh + 1) * 256], start=True, stop=False),
                r=[R("a_pts"), R("a_vbtm")], w=[R("PC0"), R("PC1")])
            for cc in range(2):
                P.do("pe", lambda e, hh=hh, cc=cc: e.matmul(
                    PC[:, hh * 256:(hh + 1) * 256],
                    lhsT=wnf[:, (hh * 2 + cc) * 128:(hh * 2 + cc + 1) * 128],
                    rhs=Sbb[:, hh * 512 + cc * 256:hh * 512 + cc * 256 + 256],
                    start=False, stop=(cc == 1)),
                    r=[R("a_wnf"), R("a_Sbb")], w=[R("PC0"), R("PC1")])
    with P.stage():
        P.do("dve", lambda e: e.tensor_copy(out=uhat[:], in_=PC[:, 0:1024]),
             r=[R("PC0"), R("PC1")], w=[R("a_uhat")])
    with P.stage():
        for hh in range(4):
            for cc in range(2):
                P.do("pe", lambda e, hh=hh, cc=cc: e.matmul(
                    half(PA, 0)[:, hh * 128:(hh + 1) * 128],
                    lhsT=ks(2 * hh + cc), rhs=qs(2 * hh + cc),
                    start=(cc == 0), stop=(cc == 1)),
                    r=[R("a_kpre"), R("a_qpre")], w=[R("PA0")])
    with P.stage():
        for hh in range(4):
            P.do("dve", lambda e, hh=hh: e.scalar_tensor_tensor(
                out=attb[:, hh * 128:(hh + 1) * 128],
                in0=half(PA, 0)[:, hh * 128:(hh + 1) * 128],
                scalar=rst[:, 4 + hh:5 + hh], in1=C.mUib[:],
                op0=ALU.mult, op1=ALU.mult),
                r=[R("PA0"), R("a_rst"), R("mUib")], w=[R("a_attb")])
    with P.stage():
        for hh in range(4):
            for cc in range(2):
                P.do("pe", lambda e, hh=hh, cc=cc: e.matmul(
                    PB[:, hh * 256:(hh + 1) * 256],
                    lhsT=qs(2 * hh + cc),
                    rhs=Sbb[:, hh * 512 + cc * 256:hh * 512 + cc * 256 + 256],
                    start=(cc == 0), stop=False),
                    r=[R("a_qpre"), R("a_Sbb")], w=[R("PB0"), R("PB1")])
            P.do("pe", lambda e, hh=hh: e.matmul(
                PB[:, hh * 256:(hh + 1) * 256],
                lhsT=attb[:, hh * 128:(hh + 1) * 128],
                rhs=uhat[:, hh * 256:(hh + 1) * 256], start=False, stop=True),
                r=[R("a_attb"), R("a_uhat")], w=[R("PB0"), R("PB1")])
    with P.stage():
        for hh in range(4):
            P.do("dve", lambda e, hh=hh: e.tensor_scalar(
                out=g["tmp_tm"][:, hh * 256:(hh + 1) * 256],
                in0=PB[:, hh * 256:(hh + 1) * 256],
                scalar1=rst[:, 0 + hh:1 + hh], scalar2=None, op0=ALU.mult),
                r=[R("PB0"), R("PB1"), R("a_rst")], w=[R("a_tmptm")])
    with P.stage():
        for hh in range(4):
            for cc in range(2):
                PP = PA if hh < 2 else PC
                pres = ("PA0", "PA1") if hh < 2 else ("PC0", "PC1")
                P.do("pe", lambda e, hh=hh, cc=cc, PP=PP: e.matmul(
                    PP[:, (hh % 2) * 512 + cc * 256:(hh % 2) * 512 + cc * 256 + 256],
                    lhsT=kntm[:, (hh * 2 + cc) * 128:(hh * 2 + cc + 1) * 128],
                    rhs=uhat[:, hh * 256:(hh + 1) * 256], start=True, stop=True),
                    r=[R("a_kntm"), R("a_uhat")], w=[R(pres[0]), R(pres[1])])
    with P.stage():
        P.do("dve", lambda e: e.tensor_tensor(out=S[:, 0:1024], in0=PA[:, 0:1024],
             in1=S[:, 0:1024], op=ALU.add),
             r=[R("PA0"), R("PA1"), R("a_S")], w=[R("a_S")])
        P.do("dve", lambda e: e.tensor_tensor(out=S[:, 1024:2048], in0=PC[:, 0:1024],
             in1=S[:, 1024:2048], op=ALU.add),
             r=[R("PC0"), R("PC1"), R("a_S")], w=[R("a_S")])
    with P.stage():
        P.do("act", lambda e: e.activation(out=Sbb[:], in_=S[:], func=AF.Identity),
             r=[R("a_S")], w=[R("a_Sbb")])
    with P.stage():
        for ch in range(8):
            P.do("pe", lambda e, ch=ch: e.transpose(
                PT[:, ch * 128:(ch + 1) * 128],
                in_=g["tmp_tm"][:, ch * 128:(ch + 1) * 128], identity=C.idb[:]),
                r=[R("a_tmptm"), R("idb")], w=[R("PT")])
    with P.stage():
        P.do("dve", lambda e, dfm=dfm: e.tensor_copy(out=dfm, in_=PT[:, 0:1024]),
             r=[R("PT")], w=[R("a_dfm2")])
    if "delta" in C.dbg and t == 0:
        with P.stage():
            P.dma(C.dbg["delta"][:], g["tmp_tm"][:, 0:1024], r=[R("a_tmptm")])
    if "dd" in C.dbg and t == 0:
        with P.stage():
            P.dma(C.dbg["dd"][:, 0:512], ATs[:], r=[R("a_ats0")])
            P.dma(C.dbg["dd"][:, 512:1024], PTs[:], r=[R("a_pts")])
            P.dma(C.dbg["dd"][:, 1024:2048], uhat[:], r=[R("a_uhat")])
            P.dma(C.dbg["dd"][:, 2048:2560], attb[:], r=[R("a_attb")])
            P.dma(C.dbg["dd"][:, 2560:3584], kntm[:], r=[R("a_kntm")])
            P.dma(C.dbg["dd"][:, 3584:4096], As[:], r=[R("a_apre")])
            P.dma(C.dbg["dd2"][:, 0:1024], g["vbtm"][:], r=[R("a_vbtm")])
            P.dma(C.dbg["dd2"][:, 1024:2048], g["kbtm"][:], r=[R("a_kbtm")])
            P.dma(C.dbg["rst"][:], g["rst"][:], r=[R("a_rst")])
    with P.stage():
        for ch in range(8):
            P.do("dve", lambda e, ch=ch, b1f=b1f: e.tensor_scalar(
                out=b1f[:, ch * 128:(ch + 1) * 128], in0=vs(ch),
                scalar1=C.wtab[:, ch * 54 + 12:ch * 54 + 13], scalar2=None,
                op0=ALU.mult), r=[R("a_vf"), R("wtab")], w=[R("a_b1f2")])
    fstk = lambda f: f[:].rearrange("p (c n) -> p c n", c=8)[:, :, o:o + 128]
    cstk = lambda b: b.rearrange("p (c n) -> p c n", c=8)
    branches = [
        (lambda ch: b1f[:, ch * 128:(ch + 1) * 128], cstk(b1f), "a_b1f2"),
        (lambda ch: f3[:, ch * 512 + o:ch * 512 + o + 128], fstk(f3), "a_f3"),
        (lambda ch: f7[:, ch * 512 + o:ch * 512 + o + 128], fstk(f7), "a_f7"),
        (lambda ch: f31[:, ch * 512 + o:ch * 512 + o + 128], fstk(f31), "a_f31"),
        (lambda ch: dfm[:, ch * 128:(ch + 1) * 128], cstk(dfm), "a_dfm2"),
        (lambda ch: vs(ch), vstack, "a_vf"),
    ]
    for br, (bsl, bstk, bres) in enumerate(branches):
        with P.stage():
            P.do("dve", lambda e, bstk=bstk: e.tensor_tensor(
                out=bsq[:].rearrange("p (c n) -> p c n", c=8),
                in0=bstk, in1=bstk, op=ALU.mult),
                r=[R(bres)], w=[R("a_bsq")])
            P.do("act", lambda e, bstk=bstk: e.activation(
                out=babs[:].rearrange("p (c n) -> p c n", c=8),
                in_=bstk, func=AF.Abs),
                r=[R(bres)], w=[R("a_babs")])
        with P.stage():
            for hh in range(4):
                col = 8 + hh * 6 + br
                for cc in range(2):
                    P.do("pe", lambda e, hh=hh, cc=cc, col=col, bsl=bsl: e.matmul(
                        PS[:, col:col + 1], lhsT=bsl(2 * hh + cc), rhs=C.onesb[:],
                        start=(cc == 0), stop=(cc == 1)),
                        r=[R(bres), R("onesb")], w=[R("PS")])
                for cc in range(2):
                    P.do("pe", lambda e, hh=hh, cc=cc, col=col: e.matmul(
                        PS[:, 32 + col - 8:32 + col - 8 + 1],
                        lhsT=babs[:, (2 * hh + cc) * 128:(2 * hh + cc + 1) * 128],
                        rhs=C.onesb[:], start=(cc == 0), stop=(cc == 1)),
                        r=[R("a_babs"), R("onesb")], w=[R("PS")])
                for cc in range(2):
                    P.do("pe", lambda e, hh=hh, cc=cc, col=col: e.matmul(
                        PS[:, 56 + col - 8:56 + col - 8 + 1],
                        lhsT=bsq[:, (2 * hh + cc) * 128:(2 * hh + cc + 1) * 128],
                        rhs=C.onesb[:], start=(cc == 0), stop=(cc == 1)),
                        r=[R("a_bsq"), R("onesb")], w=[R("PS")])
    T = C.tiny
    with P.stage():
        P.do("dve", lambda e: e.tensor_scalar(
            out=st96ap(st96, 0), in0=psblock(PS, 8),
            scalar1=1.0 / 256, scalar2=None, op0=ALU.mult),
            r=[R("PS")], w=[R("a_st96")])
        P.do("dve", lambda e: e.tensor_scalar(
            out=st96ap(st96, 2), in0=psblock(PS, 32),
            scalar1=1.0 / 256, scalar2=None, op0=ALU.mult),
            r=[R("PS")], w=[R("a_st96")])
        P.do("act", lambda e: e.activation(out=T[:, 0:24], in_=PS[:, 8:32],
             func=AF.Square), r=[R("PS")], w=[R("tinyA")])
    with P.stage():
        P.do("dve", lambda e: e.scalar_tensor_tensor(out=T[:, 0:24], in0=T[:, 0:24],
             scalar=-1.0 / 256, in1=PS[:, 56:80], op0=ALU.mult, op1=ALU.add),
             r=[R("tinyA"), R("PS")], w=[R("tinyA")])
    with P.stage():
        P.do("dve", lambda e: e.tensor_scalar(out=T[:, 0:24], in0=T[:, 0:24],
             scalar1=0.0, scalar2=None, op0=ALU.max), r=[R("tinyA")], w=[R("tinyA")])
    with P.stage():
        P.do("act", lambda e: e.activation(out=T[:, 0:24], in_=T[:, 0:24],
             func=AF.Ln, scale=1.0 / 255, bias=1e-30), r=[R("tinyA")], w=[R("tinyA")])
    with P.stage():
        P.do("act", lambda e: e.activation(out=st96ap(st96, 1), in_=T[:, 0:24],
             func=AF.Exp, scale=0.5), r=[R("tinyA")], w=[R("a_st96")])
        P.do("act", lambda e: e.activation(out=T[:, 0:24], in_=PS[:, 56:80],
             func=AF.Ln, bias=1e-30), r=[R("PS")], w=[R("tinyB")])
    with P.stage():
        P.do("act", lambda e: e.activation(out=st96ap(st96, 3), in_=T[:, 0:24],
             func=AF.Exp, scale=0.5), r=[R("tinyB")], w=[R("a_st96")])
    if "stats" in C.dbg and t == 0:
        with P.stage():
            P.do("dve", lambda e: e.tensor_copy(out=g["sot"][:, 0:96], in_=st96[:]),
                 r=[R("a_sot")], w=[R("a_sot")])
        with P.stage():
            P.dma(C.dbg["stats"][:], g["sot"][:, 0:96], r=[R("a_sot")])
    with P.stage():
        P.do("pe", lambda e: e.transpose(PT[0:96, 0:128], in_=st96[:],
             identity=C.idb[:]), r=[R("a_st96"), R("idb")], w=[R("PT")])
    with P.stage():
        P.do("dve", lambda e: e.tensor_copy(out=stT[:], in_=PT[0:96, 0:128]),
             r=[R("PT")], w=[R("a_stT")])
    with P.stage():
        for m in range(8):
            for k in range(9):
                if k < 8:
                    P.do("pe", lambda e, m=m, k=k: e.matmul(
                        C.PC[:, m * 128:(m + 1) * 128]
                        if m < 4 else C.PC[:, (m - 4) * 128 + 512:(m - 3) * 128 + 512],
                        lhsT=C.w1b[:, k * 1024 + m * 128:k * 1024 + (m + 1) * 128],
                        rhs=xnst[:, k * 512 + o:k * 512 + o + 128],
                        start=(k == 0), stop=False),
                        r=[R("w1b"), R("a_xnst")],
                        w=[R("PC0") if m < 4 else R("PC1")])
                else:
                    P.do("pe", lambda e, m=m: e.matmul(
                        C.PC[:, m * 128:(m + 1) * 128]
                        if m < 4 else C.PC[:, (m - 4) * 128 + 512:(m - 3) * 128 + 512],
                        lhsT=C.w1b[0:96, 8 * 1024 + m * 128:8 * 1024 + (m + 1) * 128],
                        rhs=stT[:], start=False, stop=True),
                        r=[R("w1b"), R("a_stT")],
                        w=[R("PC0") if m < 4 else R("PC1")])
    with P.stage():
        for m in range(8):
            P.do("dve", lambda e, m=m, sl=sl: e.tensor_scalar(
                out=hgst[:, sl * 1024 + m * 128:sl * 1024 + (m + 1) * 128],
                in0=C.PC[:, m * 128:(m + 1) * 128] if m < 4
                else C.PC[:, (m - 4) * 128 + 512:(m - 3) * 128 + 512],
                scalar1=C.b1fm[:, m:m + 1], scalar2=None, op0=ALU.add),
                r=[R("PC0") if m < 4 else R("PC1"), R("b1fm")], w=[R("a_hgst")])


def tileA_phase2(C, P, R, env):
    g = env
    st, i, t, o = g["st"], g["i"], g["t"], g["o"]
    PA, PB, PC, PS, PT = C.PA, C.PB, C.PC, C.PS, C.PT
    half = g["half"]
    sl = i % 2
    hgst, stT, l24, w24, wgtT = g["hgst"], g["stT"], g["l24"], g["w24"], g["wgtT"]
    facc, tmpc, ffm, sot = g["facc"], g["tmpc"], g["ffm"], g["sot"]
    rr2, vf, f3, f7, f31 = g["rr2"], g["vf"], g["f3"], g["f7"], g["f31"]
    dfm = g["dfm2"][:, sl * 1024:(sl + 1) * 1024]
    b1f = g["b1f2"][:, sl * 1024:(sl + 1) * 1024]
    vs = lambda ch: vf[:, ch * 544 + 32 + o:ch * 544 + 32 + o + 128]
    with P.stage():
        for cch in range(8):
            P.do("pe", lambda e, cch=cch, sl=sl: e.matmul(
                PS[:, 96:120],
                lhsT=hgst[:, sl * 1024 + cch * 128:sl * 1024 + (cch + 1) * 128],
                rhs=C.w2b[:, cch * 24:(cch + 1) * 24],
                start=(cch == 0), stop=(cch == 7)),
                r=[R("a_hgst"), R("w2b")], w=[R("PS")])
    if "hg" in C.dbg and t == 0:
        with P.stage():
            P.do("dve", lambda e: e.tensor_copy(out=g["sot"][:],
                 in_=hgst[:, 0:1024]), r=[R("a_hgst")], w=[R("a_sot")])
        with P.stage():
            P.dma(C.dbg["hg"][:], g["sot"][:], r=[R("a_sot")])
    T = C.tiny
    with P.stage():
        P.do("dve", lambda e: e.tensor_tensor(out=l24[:], in0=PS[:, 96:120],
             in1=C.invtempT[:], op=ALU.mult), r=[R("PS"), R("invtempT")],
             w=[R("a_l24")])
    with P.stage():
        P.do("dve", lambda e: e.tensor_tensor(out=l24[:], in0=l24[:],
             in1=C.b2i[:], op=ALU.add), r=[R("a_l24"), R("b2i")], w=[R("a_l24")])
    with P.stage():
        P.do("dve", lambda e: e.reduce_max(out=T[:, 0:4],
             in_=l24[:].rearrange("p (h s) -> p h s", h=4), axis=AX.X),
             r=[R("a_l24")], w=[R("tinyA")])
    with P.stage():
        for hh in range(4):
            P.do("dve", lambda e, hh=hh: e.tensor_scalar(
                out=l24[:, hh * 6:(hh + 1) * 6], in0=l24[:, hh * 6:(hh + 1) * 6],
                scalar1=T[:, hh:hh + 1], scalar2=None, op0=ALU.subtract),
                r=[R("a_l24"), R("tinyA")], w=[R("a_l24")])
    with P.stage():
        P.do("act", lambda e: e.activation(out=l24[:], in_=l24[:], func=AF.Exp),
             r=[R("a_l24")], w=[R("a_l24")])
    with P.stage():
        P.do("dve", lambda e: e.reduce_sum(out=T[:, 0:4],
             in_=l24[:].rearrange("p (h s) -> p h s", h=4), axis=AX.X),
             r=[R("a_l24")], w=[R("tinyA")])
    with P.stage():
        P.do("dve", lambda e: e.reciprocal(out=T[:, 4:8], in_=T[:, 0:4]),
             r=[R("tinyA")], w=[R("tinyA")])
    with P.stage():
        for hh in range(4):
            P.do("dve", lambda e, hh=hh: e.tensor_scalar(
                out=w24[:, hh * 6:(hh + 1) * 6], in0=l24[:, hh * 6:(hh + 1) * 6],
                scalar1=T[:, 4 + hh:5 + hh], scalar2=None, op0=ALU.mult),
                r=[R("a_l24"), R("tinyA")], w=[R("a_w24")])
    if "wgt" in C.dbg and t == 0:
        with P.stage():
            P.do("dve", lambda e: e.tensor_copy(out=l24[:], in_=w24[:]),
                 r=[R("a_w24")], w=[R("a_l24")])
        with P.stage():
            P.dma(C.dbg["wgt"][:], l24[:], r=[R("a_l24")])
    with P.stage():
        P.do("pe", lambda e: e.transpose(PT[0:24, 0:128], in_=w24[:],
             identity=C.idb[:]), r=[R("a_w24"), R("idb")], w=[R("PT")])
    with P.stage():
        P.do("dve", lambda e: e.tensor_copy(out=wgtT[:], in_=PT[0:24, 0:128]),
             r=[R("PT")], w=[R("a_wgtT")])
    fstk = lambda f: f[:].rearrange("p (c n) -> p c n", c=8)[:, :, o:o + 128]
    cstk = lambda b: b.rearrange("p (c n) -> p c n", c=8)
    vstack = vf[:].rearrange("p (c n) -> p c n", c=8)[:, :, 32 + o:32 + o + 128]
    branches = [
        (cstk(b1f), "a_b1f2"),
        (fstk(f3), "a_f3"),
        (fstk(f7), "a_f7"),
        (fstk(f31), "a_f31"),
        (cstk(dfm), "a_dfm2"),
        (vstack, "a_vf"),
    ]
    for s6, (bstk, bres) in enumerate(branches):
        PP, ppres = (PA, ("PA0", "PA1")) if s6 % 2 == 0 else (PB, ("PB0", "PB1"))
        with P.stage():
            for ch in range(8):
                hh = ch // 2
                P.do("pe", lambda e, ch=ch, hh=hh, s6=s6, PP=PP: e.matmul(
                    PP[:, ch * 128:(ch + 1) * 128],
                    lhsT=C.sel24b[:, (hh * 6 + s6) * 128:(hh * 6 + s6 + 1) * 128],
                    rhs=wgtT[:], start=True, stop=True),
                    r=[R("sel24b"), R("a_wgtT")], w=[R(ppres[ch // 4])])
        with P.stage():
            dst, dres = (facc, "a_facc") if s6 == 0 else (tmpc, "a_tmpc")
            P.do("dve", lambda e, bstk=bstk, PP=PP, dst=dst: e.tensor_tensor(
                out=dst[:].rearrange("p (c n) -> p c n", c=8),
                in0=bstk, in1=PP[:, 0:1024].rearrange("p (c n) -> p c n", c=8),
                op=ALU.mult),
                r=[R(bres), R(ppres[0]), R(ppres[1])], w=[R(dres)])
        if s6 > 0:
            with P.stage():
                P.do("pool", lambda e: e.tensor_tensor(out=facc[:], in0=facc[:],
                     in1=tmpc[:], op=ALU.add),
                     r=[R("a_facc"), R("a_tmpc")], w=[R("a_facc")])
    with P.stage():
        P.do("dve", lambda e: e.tensor_tensor(out=tmpc[:], in0=facc[:],
             in1=facc[:], op=ALU.mult), r=[R("a_facc")], w=[R("a_tmpc")])
    with P.stage():
        for hh in range(4):
            for cc in range(2):
                P.do("pe", lambda e, hh=hh, cc=cc: e.matmul(
                    PS[:, 120 + hh:121 + hh],
                    lhsT=tmpc[:, (2 * hh + cc) * 128:(2 * hh + cc + 1) * 128],
                    rhs=C.onesb[:], start=(cc == 0), stop=(cc == 1)),
                    r=[R("a_tmpc"), R("onesb")], w=[R("PS")])
    with P.stage():
        P.do("act", lambda e, sl=sl: e.activation(out=rr2[:, sl * 4:(sl + 1) * 4],
             in_=PS[:, 120:124], func=AF.Ln, scale=1.0 / 256, bias=1e-5),
             r=[R("PS")], w=[R("a_rr2")])
    with P.stage():
        P.do("act", lambda e, sl=sl: e.activation(out=rr2[:, sl * 4:(sl + 1) * 4],
             in_=rr2[:, sl * 4:(sl + 1) * 4], func=AF.Exp, scale=-0.5),
             r=[R("a_rr2")], w=[R("a_rr2")])
    with P.stage():
        P.do("dve", lambda e, sl=sl: e.tensor_copy(out=w24[:, 0:4],
             in_=rr2[:, sl * 4:(sl + 1) * 4]), r=[R("a_rr2")], w=[R("a_w24")])
    with P.stage():
        for hh in range(4):
            P.do("pe", lambda e, hh=hh: e.transpose(
                PT[0:1, hh * 128:(hh + 1) * 128], in_=w24[:, hh:hh + 1],
                identity=C.idb[:]), r=[R("a_w24"), R("idb")], w=[R("PT")])
    with P.stage():
        P.do("dve", lambda e: e.tensor_copy(out=g["rrow"][0:1, 0:512],
             in_=PT[0:1, 0:512]), r=[R("PT")], w=[R("a_rrow")])
    with P.stage():
        for hh in range(4):
            P.do("pe", lambda e, hh=hh: e.matmul(
                half(C.PA, 0)[:, hh * 128:(hh + 1) * 128], lhsT=C.onesrow[:],
                rhs=g["rrow"][0:1, hh * 128:(hh + 1) * 128],
                start=True, stop=True),
                r=[R("a_rrow"), R("onesrow")], w=[R("PA0")])
    with P.stage():
        for ch in range(8):
            P.do("dve", lambda e, ch=ch: e.tensor_tensor(
                out=ffm[:, ch * 128:(ch + 1) * 128],
                in0=facc[:, ch * 128:(ch + 1) * 128],
                in1=half(C.PA, 0)[:, (ch // 2) * 128:(ch // 2 + 1) * 128],
                op=ALU.mult),
                r=[R("a_facc"), R("PA0")], w=[R("a_ffm")])
    with P.stage():
        for k in range(8):
            for nn in range(2):
                P.do("pe", lambda e, k=k, nn=nn: e.matmul(
                    PC[:, nn * 512:(nn + 1) * 512],
                    lhsT=ffm[:, k * 128:(k + 1) * 128],
                    rhs=C.owt[:, k * 1024 + nn * 512:k * 1024 + (nn + 1) * 512],
                    start=(k == 0), stop=(k == 7)),
                    r=[R("a_ffm"), R("owt")], w=[R("PC%d" % nn)])
    with P.stage(dma_out=True):
        P.do("dve", lambda e: e.tensor_copy(out=sot[:], in_=PC[:, 0:1024]),
             r=[R("PC0"), R("PC1")], w=[R("a_sot")])
    with P.stage():
        P.dma(C.s_s[t], sot[:], r=[R("a_sot")])
        if "s" in C.dbg and t == 0:
            P.dma(C.dbg["s"][:], sot[:], r=[R("a_sot")])


def st96ap(st96, stat):
    return st96[:].rearrange("p (x f) -> p x f", f=4)[:, :, stat:stat + 1]


def psblock(PS, base):
    return PS[:, base:base + 24]


def phaseB(C, es):
    nc, P = C.nc, C.P
    R = P.res
    sb = lambda n, sh, dt=F32: es.enter_context(nc.sbuf_tensor(n, sh, dt))
    wgu = sb("b_wgu", [128, 8 * 6144], BF16)
    wdb = sb("b_wdb", [128, 24 * 1024], BF16)
    xnb = [sb("b_xn0", [128, 1024], BF16), sb("b_xn1", [128, 1024], BF16)]
    xb = [sb("b_x0", [128, 1024]), sb("b_x1", [128, 1024])]
    sbf = [sb("b_s0", [128, 1024]), sb("b_s1", [128, 1024])]
    sg = sb("b_sg", [128, 512], BF16)
    hfm = sb("b_hfm", [128, 24 * 128], BF16)
    gb = sb("b_gb", [128, 1024], BF16)
    tm = sb("b_tm", [128, 1024])
    ob = [sb("b_ob0", [128, 1024]), sb("b_ob1", [128, 1024])]
    PA, PB, PC, PS = C.PA, C.PB, C.PC, C.PS
    with P.stage():
        P.dma(wgu[:].rearrange("p (k n) -> p k n", k=8),
              C.wgu_s.rearrange("k p n -> p k n"), w=[R("b_wgu")])
        P.dma(wdb[:].rearrange("p (k n) -> p k n", k=24),
              C.wdT_s.rearrange("k p n -> p k n"), w=[R("b_wdb")])
    for t in range(NT):
        p = t % 2
        with P.stage():
            P.dma(xnb[p][:], C.xn_s[t], w=[R("b_xn%d" % p)])
            P.dma(xb[p][:], C.x_h[t * 128:(t + 1) * 128, :], w=[R("b_x%d" % p)])
            P.dma(sbf[p][:], C.s_s[t], w=[R("b_s%d" % p)])
        for q4 in range(6):
            with P.stage():
                for mm in range(4):
                    m = q4 * 4 + mm
                    for k in range(8):
                        P.do("pe", lambda e, m=m, mm=mm, k=k, p=p: e.matmul(
                            PA[:, mm * 128:(mm + 1) * 128],
                            lhsT=wgu[:, k * 6144 + m * 128:k * 6144 + (m + 1) * 128],
                            rhs=xnb[p][:, k * 128:(k + 1) * 128],
                            start=(k == 0), stop=(k == 7)),
                            r=[R("b_wgu"), R("b_xn%d" % p)], w=[R("PA0")])
                    for k in range(8):
                        P.do("pe", lambda e, m=m, mm=mm, k=k, p=p: e.matmul(
                            PB[:, mm * 128:(mm + 1) * 128],
                            lhsT=wgu[:, k * 6144 + (24 + m) * 128:k * 6144 + (25 + m) * 128],
                            rhs=xnb[p][:, k * 128:(k + 1) * 128],
                            start=(k == 0), stop=(k == 7)),
                            r=[R("b_wgu"), R("b_xn%d" % p)], w=[R("PB0")])
            with P.stage():
                P.do("act", lambda e: e.activation(out=sg[:], in_=PA[:, 0:512],
                     func=AF.Silu, scale=1.0 / 127), r=[R("PA0")],
                     w=[R("b_sg")])
            with P.stage():
                P.do("dve", lambda e, q4=q4: e.tensor_tensor(
                    out=hfm[:, q4 * 512:(q4 + 1) * 512], in0=PB[:, 0:512],
                    in1=sg[:], op=ALU.mult),
                    r=[R("PB0"), R("b_sg")], w=[R("b_hfm")])
        with P.stage():
            for cch in range(24):
                for nn in range(2):
                    P.do("pe", lambda e, cch=cch, nn=nn: e.matmul(
                        PC[:, nn * 512:(nn + 1) * 512],
                        lhsT=hfm[:, cch * 128:(cch + 1) * 128],
                        rhs=wdb[:, cch * 1024 + nn * 512:cch * 1024 + (nn + 1) * 512],
                        start=(cch == 0), stop=(cch == 23)),
                        r=[R("b_hfm"), R("b_wdb")], w=[R("PC%d" % nn)])
        with P.stage():
            P.do("act", lambda e: e.activation(out=gb[:], in_=PC[:, 0:1024],
                 func=AF.Silu, scale=1.0 / (127.0 * 127.0)),
                 r=[R("PC0"), R("PC1")], w=[R("b_gb")])
        if "g" in C.dbg and t == 0:
            with P.stage():
                P.do("dve", lambda e: e.tensor_copy(out=tm[:], in_=gb[:]),
                     r=[R("b_gb")], w=[R("b_tm")])
            with P.stage():
                P.dma(C.dbg["g"][:], tm[:], r=[R("b_tm")])
        with P.stage():
            P.do("dve", lambda e, p=p: e.tensor_tensor(out=tm[:], in0=sbf[p][:],
                 in1=gb[:], op=ALU.mult), r=[R("b_s%d" % p), R("b_gb")],
                 w=[R("b_tm")])
        with P.stage(dma_out=True):
            P.do("pool", lambda e, p=p: e.tensor_tensor(out=ob[p][:], in0=tm[:],
                 in1=xb[p][:], op=ALU.add), r=[R("b_tm"), R("b_x%d" % p)],
                 w=[R("b_ob%d" % p)])
        with P.stage():
            P.dma(C.out_h[t * 128:(t + 1) * 128, :], ob[p][:], r=[R("b_ob%d" % p)])


def kernel(**inputs):
    hp = host_prep(inputs)
    nc, dbg_names = build_program()
    x_full = np.asarray(inputs["L0_attn_hidden_states"])
    in_maps = []
    for c in range(N_CORES):
        m = dict(hp)
        m["x"] = np.ascontiguousarray(x_full[c])
        in_maps.append(m)
    res = run_bass_kernel_spmd(nc, in_maps, core_ids=list(range(N_CORES)))
    out = np.stack([res.results[c]["out"] for c in range(N_CORES)], axis=0)
    if DEBUG:
        kernel._dbg = [{k: res.results[c].get("dbg_" + k) for k in DEBUG}
                       for c in range(N_CORES)]
    return out
